# version 01
# baseline (speedup 1.0000x reference)
import numpy as np
import ml_dtypes
from contextlib import ExitStack

import concourse.bass as bass
import concourse.mybir as mybir
from concourse.bass_utils import run_bass_kernel_spmd

F32 = mybir.dt.float32
BF16 = mybir.dt.bfloat16
I32 = mybir.dt.int32
AF = mybir.ActivationFunctionType
OP = mybir.AluOpType
AX = mybir.AxisListType

NCORE = 8
D = 2048
TPC = 2048
NDC = D // 128


class Prog:
    ENGINES = ("tensor", "vector", "scalar", "gpsimd", "sync")
    AUTO = ("vector", "scalar", "gpsimd")

    def __init__(self, nc, stack):
        self.nc = nc
        self.stack = stack
        self.ops = {e: [] for e in self.ENGINES}
        self.sems = {}
        self.counts = {}

    def sem(self, name):
        if name not in self.sems:
            self.sems[name] = self.stack.enter_context(self.nc.semaphore(name))
            self.counts[name] = 0
        return name

    def add(self, engine, fn, waits=(), inc=None, amt=1, auto=True, tag=""):
        waits = [w for w in waits if w is not None]
        ev = None
        if engine in self.AUTO and auto:
            p = self.sem("p_" + engine)
            if self.counts[p] > 0:
                waits.append((p, self.counts[p]))
            if inc is None:
                inc, amt = p, 1
        if inc is not None:
            self.sem(inc)
            self.counts[inc] += amt
            ev = (inc, self.counts[inc])
        self.ops[engine].append((fn, waits, inc, amt, tag))
        return ev

    def dma(self, engine, out, in_, sem, waits=(), tag="", **kw):
        def fn(eng, out=out, in_=in_, kw=kw):
            return eng.dma_start(out=out, in_=in_, **kw)
        return self.add(engine, fn, waits=waits, inc=self.sem(sem), amt=16,
                        auto=False, tag=tag)

    @staticmethod
    def ap(x):
        return x if isinstance(x, bass.AP) else x[:, :]

    def mm(self, psA, lhsT, rhs, start, stop, waits=(), inc=None, tag=""):
        psA, lhsT, rhs = self.ap(psA), self.ap(lhsT), self.ap(rhs)
        def fn(eng, psA=psA, lhsT=lhsT, rhs=rhs, start=start, stop=stop):
            return eng.matmul(psA, lhsT, rhs, start=start, stop=stop)
        return self.add("tensor", fn, waits=waits, inc=inc, tag=tag)

    def vev(self):
        return ("p_vector", self.counts.get("p_vector", 0))

    def check(self):
        idx = {e: 0 for e in self.ENGINES}
        counts = {s: 0 for s in self.sems}
        total = sum(len(v) for v in self.ops.values())
        done = 0
        while done < total:
            progressed = False
            for e in self.ENGINES:
                while idx[e] < len(self.ops[e]):
                    fn, waits, inc, amt, tag = self.ops[e][idx[e]]
                    if all(counts[s] >= v for s, v in waits):
                        if inc is not None:
                            counts[inc] += amt
                        idx[e] += 1
                        done += 1
                        progressed = True
                    else:
                        break
            if not progressed:
                for e in self.ENGINES:
                    if idx[e] < len(self.ops[e]):
                        fn, waits, inc, amt, tag = self.ops[e][idx[e]]
                        unmet = [(s, v, counts[s]) for s, v in waits if counts[s] < v]
                        raise RuntimeError(
                            f"deadlock: engine {e} op {idx[e]} tag={tag} waits {unmet}")

    def emit(self):
        self.check()
        nc = self.nc
        with nc.Block() as block:
            for e in self.ENGINES:
                ops = self.ops[e]
                if not ops:
                    continue
                def body(eng, ops=ops):
                    for fn, waits, inc, amt, tag in ops:
                        for s, v in waits:
                            eng.wait_ge(self.sems[s], v)
                        instr = fn(eng)
                        if inc is not None:
                            if instr is None:
                                instr = eng.nop()
                            instr.then_inc(self.sems[inc], amt)
                getattr(block, e)(body)


class Ctx:
    def __init__(self, P, stack):
        nc = P.nc
        self.P = P
        self.ones_col = stack.enter_context(nc.sbuf_tensor("ones_col", [128, 1], BF16))
        self.ones_row = stack.enter_context(nc.sbuf_tensor("ones_row", [1, 128], BF16))
        self.ones_row32 = stack.enter_context(nc.sbuf_tensor("ones_row32", [1, 128], F32))
        self.ident = stack.enter_context(nc.sbuf_tensor("ident", [128, 128], BF16))
        self.consts = stack.enter_context(nc.sbuf_tensor("cconsts", [1, 8], F32))
        self.pb = [stack.enter_context(nc.psum_tensor(f"pb{i}", [128, 512], F32))
                   for i in range(7)]
        self.pt = stack.enter_context(nc.psum_tensor("pt", [128, 1024], BF16))
        e1 = P.add("gpsimd", lambda eng: eng.memset(self.ones_col[:, :], 1.0), tag="c1")
        P.add("gpsimd", lambda eng: eng.memset(self.consts[:, 0:1], 1e-5), tag="ceps")
        P.add("gpsimd", lambda eng: eng.memset(self.consts[:, 1:2], 1e-6), tag="ceps2")
        e2 = P.add("gpsimd", lambda eng: eng.memset(self.ones_row[:, :], 1.0), tag="c2")
        e3 = P.add("gpsimd", lambda eng: eng.memset(self.ones_row32[:, :], 1.0), tag="c3")
        P.add("gpsimd", lambda eng: eng.memset(self.ident[:, :], 0.0), tag="c4a")
        self.const_ev = P.add("gpsimd", lambda eng: eng.affine_select(
            out=self.ident[:, :], in_=self.ident[:, :],
            compare_op=OP.not_equal, fill=1.0, base=0,
            pattern=[[-1, 128]], channel_multiplier=1), tag="c4b")


def emit_ln(P, cx, xsrc, xbf, sb, pfx, T=TPC, first_waits=()):
    nTB = T // 512
    pbA, pbB, pbC = cx.pb[0], cx.pb[1], cx.pb[2]
    sq = sb["ln_sq"]
    stats = sb["ln_stats"]
    bcast = sb["ln_bcast"]

    for c in range(NDC):
        ld_all = P.dma("gpsimd", xbf[:, c * T:(c + 1) * T], xsrc[c * 128:(c + 1) * 128, :],
                       sem=pfx + "ld", waits=list(first_waits), tag=pfx + f"ldx{c}")

    mrow = lambda tb: stats[:, 0 * 512: 1 * 512]
    msq = lambda tb: stats[:, 1 * 512: 2 * 512]
    var = lambda tb: stats[:, 2 * 512: 3 * 512]
    rstd = lambda tb: stats[:, 3 * 512: 4 * 512]
    mr = lambda tb: stats[:, 4 * 512: 5 * 512]

    prev_stats_read = None
    prev_bM = None
    prev_mq = None
    for tb in range(nTB):
        mm_ev = None
        for c in range(NDC):
            csl = slice(c * T + tb * 512, c * T + tb * 512 + 512)
            w = [ld_all, cx.const_ev]
            if c == 0 and prev_stats_read is not None:
                w.append(prev_stats_read)
            P.mm(pbA[0:1, :], cx.ones_col, xbf[:, csl], start=(c == 0), stop=(c == NDC - 1),
                 waits=w, inc=P.sem(pfx + "mm") if c == NDC - 1 else None, tag=pfx + "ms")
            sev = P.add("scalar",
                        lambda eng, csl=csl: eng.activation(out=sq[:, :], in_=xbf[:, csl],
                                                            func=AF.Square),
                        waits=[ld_all, prev_mq], tag=pfx + "sq")
            wq = [sev]
            if c == 0 and prev_stats_read is not None:
                wq.append(prev_stats_read)
            mm_ev = prev_mq = P.mm(pbB[0:1, :], cx.ones_col, sq[:, :],
                                   start=(c == 0), stop=(c == NDC - 1),
                                   waits=wq, inc=P.sem(pfx + "mm"), tag=pfx + "mq")

        P.add("vector", lambda eng, tb=tb: eng.tensor_scalar(
            out=mrow(tb), in0=pbA[0:1, 0:512], scalar1=1.0 / D, scalar2=None, op0=OP.mult),
            waits=[mm_ev, prev_bM], tag=pfx + "m")
        P.add("vector", lambda eng, tb=tb: eng.tensor_scalar(
            out=msq(tb), in0=pbB[0:1, 0:512], scalar1=1.0 / D, scalar2=None, op0=OP.mult),
            tag=pfx + "q")
        P.add("vector", lambda eng, tb=tb: eng.tensor_tensor(
            out=var(tb), in0=mrow(tb), in1=mrow(tb), op=OP.mult), tag=pfx + "m2")
        vev = P.add("vector", lambda eng, tb=tb: eng.tensor_tensor(
            out=var(tb), in0=msq(tb), in1=var(tb), op=OP.subtract), tag=pfx + "v")
        prev_stats_read = vev
        aev = P.add("scalar", lambda eng, tb=tb: eng.activation(
            out=rstd(tb), in_=var(tb), func=AF.Ln, bias=cx.consts[0:1, 0:1]),
            waits=[vev, cx.const_ev], tag=pfx + "l")
        aev = P.add("scalar", lambda eng, tb=tb: eng.activation(
            out=rstd(tb), in_=rstd(tb), func=AF.Exp, scale=-0.5), tag=pfx + "e")
        mrev = P.add("vector", lambda eng, tb=tb: eng.tensor_tensor(
            out=mr(tb), in0=mrow(tb), in1=rstd(tb), op=OP.mult), waits=[aev], tag=pfx + "r")

        bev1 = P.mm(pbC, cx.ones_row32, rstd(tb), True, True, waits=[aev],
                    inc=P.sem(pfx + "mm"), tag=pfx + "bR")
        cev1 = P.add("vector", lambda eng: eng.tensor_copy(out=bcast[:, 0:512], in_=pbC[:, :]),
                     waits=[bev1], tag=pfx + "cR")
        bev2 = P.mm(pbC, cx.ones_row32, mr(tb), True, True, waits=[mrev, cev1],
                    inc=P.sem(pfx + "mm"), tag=pfx + "bM")
        prev_bM = bev2
        cev2 = P.add("vector", lambda eng: eng.tensor_copy(out=bcast[:, 512:1024], in_=pbC[:, :]),
                     waits=[bev2], tag=pfx + "cM")

        for c in range(NDC):
            csl = slice(c * T + tb * 512, c * T + tb * 512 + 512)
            P.add("vector", lambda eng, csl=csl: eng.tensor_tensor(
                out=xbf[:, csl], in0=xbf[:, csl], in1=bcast[:, 0:512], op=OP.mult),
                tag=pfx + "n1")
            P.add("vector", lambda eng, csl=csl: eng.tensor_tensor(
                out=xbf[:, csl], in0=xbf[:, csl], in1=bcast[:, 512:1024], op=OP.subtract),
                tag=pfx + "n2")
    return P.vev()


def build_ffn0():
    nc = bass.Bass("TRN2", target_bir_lowering=False)
    stack = ExitStack()
    P = Prog(nc, stack)
    T = TPC

    xres = nc.dram_tensor("xres", [D, T], F32, kind="ExternalInput").ap()
    gwT = nc.dram_tensor("gwT", [D, 8], F32, kind="ExternalInput").ap()
    WG = nc.dram_tensor("WG", [8, D, 1408], F32, kind="ExternalInput").ap()
    WU = nc.dram_tensor("WU", [8, D, 1408], F32, kind="ExternalInput").ap()
    WD = nc.dram_tensor("WD", [8, 1408, D], F32, kind="ExternalInput").ap()
    SG = nc.dram_tensor("SG", [D, 2816], F32, kind="ExternalInput").ap()
    SU = nc.dram_tensor("SU", [D, 2816], F32, kind="ExternalInput").ap()
    SD = nc.dram_tensor("SD", [2816, D], F32, kind="ExternalInput").ap()
    xout = nc.dram_tensor("xout", [D, T], F32, kind="ExternalOutput").ap()
    ydram = nc.dram_tensor("ydram", [D, T], F32, kind="Internal").ap()

    with stack:
        cx = Ctx(P, stack)
        sb = {
            "ln_sq": stack.enter_context(nc.sbuf_tensor("ln_sq", [128, 512], BF16)),
            "ln_stats": stack.enter_context(nc.sbuf_tensor("ln_stats", [1, 5 * 512], F32)),
            "ln_bcast": stack.enter_context(nc.sbuf_tensor("ln_bcast", [128, 1024], F32)),
        }
        xbf = stack.enter_context(nc.sbuf_tensor("xbf", [128, NDC * T], BF16))
        gw_sb = stack.enter_context(nc.sbuf_tensor("gw_sb", [128, NDC * 8], BF16))
        rtmp = stack.enter_context(nc.sbuf_tensor("rtmp", [128, 48], F32))
        rcol = stack.enter_context(nc.sbuf_tensor("rcol", [128, 4], F32))
        wbf = stack.enter_context(nc.sbuf_tensor("wbf", [128, 8], BF16))
        wT = stack.enter_context(nc.sbuf_tensor("wT", [8, NDC * 128], BF16))
        selm = stack.enter_context(nc.sbuf_tensor("selm", [8, 8 * 128], BF16))
        wb_e = stack.enter_context(nc.sbuf_tensor("wb_e", [128, T], BF16))
        gub = [stack.enter_context(nc.sbuf_tensor(f"gub{i}", [128, NDC * 128], BF16))
               for i in range(4)]
        dbuf = stack.enter_context(nc.sbuf_tensor("dbuf", [128, 11 * D], BF16))
        tsil = stack.enter_context(nc.sbuf_tensor("tsil", [128, 512], BF16))
        ttil = stack.enter_context(nc.sbuf_tensor("ttil", [128, 11 * T], BF16))
        ybuf = stack.enter_context(nc.sbuf_tensor("ybuf", [128, T], F32))
        xold = stack.enter_context(nc.sbuf_tensor("xold", [128, 512], F32))

        for e in range(8):
            sl = selm[:, e * 128:(e + 1) * 128]
            P.add("gpsimd", lambda eng, sl=sl: eng.memset(sl, 0.0), tag=f"sla{e}")
            sel_ev = P.add("gpsimd", lambda eng, sl=sl, e=e: eng.affine_select(
                out=sl, in_=sl, compare_op=OP.not_equal, fill=1.0, base=-e,
                pattern=[[0, 128]], channel_multiplier=1), tag=f"slb{e}")

        gw_evs = []
        for c in range(NDC):
            gw_evs.append(P.dma("gpsimd", gw_sb[:, c * 8:(c + 1) * 8],
                                gwT[c * 128:(c + 1) * 128, :], sem="ldw", tag=f"gw{c}"))
        gw_ev = gw_evs[-1]

        ln_ev = emit_ln(P, cx, xres, xbf, sb, "ln")

        pbR = cx.pb[3]
        s = rtmp[:, 0:8]; e1 = rtmp[:, 8:16]; sm = rtmp[:, 16:24]
        s2 = rtmp[:, 24:32]; keep = rtmp[:, 32:40]; wr = rtmp[:, 40:48]
        m1 = rcol[:, 0:1]; m2 = rcol[:, 1:2]; den = rcol[:, 2:3]; rden = rcol[:, 3:4]
        prev_r = None
        for tc in range(NDC):
            mm_ev = None
            for c in range(NDC):
                w = [ln_ev, gw_ev] if c == 0 else []
                if c == 0 and prev_r is not None:
                    w.append(prev_r)
                mm_ev = P.mm(pbR[:, 0:8], xbf[:, c * T + tc * 128: c * T + tc * 128 + 128],
                             gw_sb[:, c * 8:(c + 1) * 8], start=(c == 0), stop=(c == NDC - 1),
                             waits=w, inc=P.sem("rmm") if c == NDC - 1 else None, tag=f"rt{tc}")
            P.add("vector", lambda eng: eng.reduce_max(m1, pbR[:, 0:8], axis=AX.X),
                  waits=[mm_ev], tag="rmax")
            P.add("vector", lambda eng: eng.tensor_scalar(
                out=s, in0=pbR[:, 0:8], scalar1=m1, scalar2=None, op0=OP.subtract), tag="rsub")
            prev_r = P.vev()
            aev = P.add("scalar", lambda eng: eng.activation(out=e1, in_=s, func=AF.Exp),
                        waits=[prev_r], tag="rexp")
            P.add("vector", lambda eng: eng.reduce_sum(den, e1, axis=AX.X), waits=[aev],
                  tag="rden")
            P.add("vector", lambda eng: eng.reciprocal(rden, den), tag="rrec")
            P.add("vector", lambda eng: eng.tensor_scalar(
                out=sm, in0=e1, scalar1=rden, scalar2=None, op0=OP.mult), tag="rsm")
            P.add("vector", lambda eng: eng.reduce_max(m1, sm, axis=AX.X), tag="rm1")
            P.add("vector", lambda eng: eng.tensor_scalar(
                out=s2, in0=sm, scalar1=m1, scalar2=None, op0=OP.is_lt), tag="rlt")
            P.add("vector", lambda eng: eng.tensor_tensor(out=s2, in0=s2, in1=sm, op=OP.mult),
                  tag="rs2")
            P.add("vector", lambda eng: eng.reduce_max(m2, s2, axis=AX.X), tag="rm2")
            P.add("vector", lambda eng: eng.tensor_scalar(
                out=keep, in0=sm, scalar1=m2, scalar2=None, op0=OP.is_ge), tag="rge")
            P.add("vector", lambda eng: eng.tensor_tensor(out=wr, in0=keep, in1=sm, op=OP.mult),
                  tag="rwr")
            P.add("vector", lambda eng: eng.reduce_sum(den, wr, axis=AX.X), tag="rds")
            P.add("vector", lambda eng: eng.tensor_scalar(
                out=den, in0=den, scalar1=1e-20, scalar2=None, op0=OP.add), tag="rd2")
            P.add("vector", lambda eng: eng.reciprocal(rden, den), tag="rr2")
            vev = P.add("vector", lambda eng: eng.tensor_scalar(
                out=wbf[:, :], in0=wr, scalar1=rden, scalar2=None, op0=OP.mult), tag="rw")
            tev = P.add("tensor",
                        lambda eng: eng.transpose(cx.pt[0:8, 0:128], wbf[:, :], cx.ident[:, :]),
                        waits=[vev, cx.const_ev], inc=P.sem("rmm"), tag="rT")
            P.add("vector", lambda eng, tc=tc: eng.tensor_copy(
                out=wT[:, tc * 128:(tc + 1) * 128], in_=cx.pt[0:8, 0:128]),
                waits=[tev], tag="rTc")
        router_ev = P.vev()

        pbG, pbU, pbW = cx.pb[0], cx.pb[1], cx.pb[2]
        pbY = [cx.pb[4], cx.pb[5]]
        til_all = []
        down_done = None
        ylast = {}
        y_dma_prev = None
        ycopy = {}
        prev_sil = None
        prev_tt = None
        wb_prev = None

        for e in range(10):
            shared = (e >= 8)
            NI = 11
            ioff = 1408 * (e - 8) if shared else 0
            gsrc = (SG if shared else WG[e]).rearrange("(c p) i -> p c i", p=128)
            usrc = (SU if shared else WU[e]).rearrange("(c p) i -> p c i", p=128)
            dsrc = SD if shared else WD[e]

            if not shared:
                prev_wc = None
                for tb in range(4):
                    bev = P.mm(pbW, selm[:, e * 128:(e + 1) * 128],
                               wT[0:8, tb * 512:(tb + 1) * 512],
                               True, True, waits=[router_ev, wb_prev, sel_ev, prev_wc],
                               inc=P.sem("emm"), tag=f"wb{e}")
                    prev_wc = P.add("vector", lambda eng, tb=tb: eng.tensor_copy(
                        out=wb_e[:, tb * 512:(tb + 1) * 512], in_=pbW[:, :]),
                        waits=[bev], tag=f"wc{e}")
                wb_ev = P.vev()
            else:
                wb_ev = None

            d_evs = []
            for ic in range(NI):
                ev = P.dma("gpsimd", dbuf[:, ic * D:(ic + 1) * D],
                           dsrc[ioff + ic * 128:ioff + (ic + 1) * 128, :], sem="ldd",
                           waits=[down_done] if ic == 0 else [], tag=f"d{e}_{ic}")
                d_evs.append(ev)

            for ic in range(NI):
                bsel = ic % 2
                gb, ub = gub[2 * bsel], gub[2 * bsel + 1]
                w_prev = til_all[-2] if len(til_all) >= 2 else None
                assert w_prev is None or True
                gev1 = P.dma("gpsimd", gb.rearrange("p (c i) -> p c i", i=128),
                             gsrc[:, :, ioff + ic * 128:ioff + (ic + 1) * 128],
                             sem=f"ldg{bsel}", waits=[w_prev], tag=f"g{e}_{ic}")
                gev2 = P.dma("gpsimd", ub.rearrange("p (c i) -> p c i", i=128),
                             usrc[:, :, ioff + ic * 128:ioff + (ic + 1) * 128],
                             sem=f"ldu{bsel}", waits=[w_prev], tag=f"u{e}_{ic}")
                guev = None
                for tb in range(4):
                    tsl = slice(tb * 512, tb * 512 + 512)
                    for c in range(NDC):
                        wg = [gev1, gev2, ln_ev, prev_sil] if c == 0 else []
                        wu = [gev1, gev2, prev_tt] if c == 0 else []
                        P.mm(pbG, gb[:, c * 128:(c + 1) * 128],
                             xbf[:, c * T + tb * 512:c * T + tb * 512 + 512],
                             start=(c == 0), stop=(c == NDC - 1), waits=wg, tag=f"mg{e}")
                        mm_ev = P.mm(pbU, ub[:, c * 128:(c + 1) * 128],
                                     xbf[:, c * T + tb * 512:c * T + tb * 512 + 512],
                                     start=(c == 0), stop=(c == NDC - 1), waits=wu,
                                     inc=P.sem("emm") if c == NDC - 1 else None, tag=f"mu{e}")
                    prev_sil = P.add("scalar", lambda eng: eng.activation(
                        out=tsil[:, :], in_=pbG[:, :], func=AF.Silu), waits=[mm_ev],
                        tag=f"si{e}")
                    osl = slice(ic * T + tb * 512, ic * T + tb * 512 + 512)
                    if shared:
                        tev = P.add("vector", lambda eng, osl=osl: eng.tensor_tensor(
                            out=ttil[:, osl], in0=tsil[:, :], in1=pbU[:, :], op=OP.mult),
                            waits=[prev_sil], tag=f"tt{e}")
                    else:
                        P.add("vector", lambda eng: eng.tensor_tensor(
                            out=tsil[:, :], in0=tsil[:, :], in1=pbU[:, :], op=OP.mult),
                            waits=[prev_sil, wb_ev], tag=f"tp{e}")
                        tev = P.add("vector", lambda eng, osl=osl, tsl=tsl: eng.tensor_tensor(
                            out=ttil[:, osl], in0=tsil[:, :], in1=wb_e[:, tsl], op=OP.mult),
                            tag=f"tw{e}")
                    prev_tt = tev
                til_all.append(tev)
            til_done = til_all[-1]
            wb_prev = til_done

            for fc in range(NDC):
                for tb in range(4):
                    bi = (fc * 4 + tb) % 2
                    bank = pbY[bi]
                    for ic in range(NI):
                        w = []
                        if ic == 0:
                            w = [til_done, d_evs[-1], ycopy.get(bi)]
                        P.mm(bank, dbuf[:, ic * D + fc * 128: ic * D + fc * 128 + 128],
                             ttil[:, ic * T + tb * 512: ic * T + tb * 512 + 512],
                             start=(ic == 0), stop=(ic == NI - 1), waits=w,
                             inc=P.sem("emm") if ic == NI - 1 else None, tag=f"md{e}")
                    mm_ev = ("emm", P.counts["emm"])
                    w = [mm_ev]
                    if tb == 0:
                        w.append(y_dma_prev)
                    ycopy[bi] = P.add("vector", lambda eng, tb=tb, bank=bank: eng.tensor_copy(
                        out=ybuf[:, tb * 512:(tb + 1) * 512], in_=bank[:, :]),
                        waits=w, tag=f"yc{e}")
                yv_ev = P.vev()
                if e == 0:
                    ylast[fc] = P.dma("gpsimd", ydram[fc * 128:(fc + 1) * 128, :], ybuf[:, :],
                                      sem=f"sty{fc}", waits=[yv_ev], tag=f"y{e}_{fc}")
                else:
                    ylast[fc] = P.dma("gpsimd", ydram[fc * 128:(fc + 1) * 128, :], ybuf[:, :],
                                      sem=f"sty{fc}", waits=[yv_ev, ylast[fc]],
                                      accum_op=OP.add, tag=f"y{e}_{fc}")
                y_dma_prev = ylast[fc]
            down_done = ("emm", P.counts["emm"])

        y_final = y_dma_prev
        prev_store = None
        for fc in range(NDC):
            for tb in range(4):
                w = [prev_store] if prev_store else []
                tsl = slice(tb * 512, tb * 512 + 512)
                lev1 = P.dma("sync", xold[:, :], xres[fc * 128:(fc + 1) * 128, tsl], sem="ldr1",
                             waits=w, tag=f"lr{fc}")
                lev2 = P.dma("sync", ybuf[:, tsl], ydram[fc * 128:(fc + 1) * 128, tsl],
                             sem="ldr2", waits=[ylast[fc], y_final] + w, tag=f"ly{fc}")
                rev = P.add("vector", lambda eng, tsl=tsl: eng.scalar_tensor_tensor(
                    out=xold[:, :], in0=ybuf[:, tsl], scalar=0.5, in1=xold[:, :],
                    op0=OP.mult, op1=OP.add), waits=[lev1, lev2], tag=f"rs{fc}")
                prev_store = P.dma("sync", xout[fc * 128:(fc + 1) * 128, tsl], xold[:, :],
                                   sem="stx", waits=[rev], tag=f"sx{fc}")
        P.add("scalar", lambda eng: eng.nop(), waits=[prev_store], tag="fin")

        P.emit()
    return nc


def _shard_x(x):
    out = []
    for r in range(NCORE):
        a = x[0, 1024 * r:1024 * (r + 1), :].T
        b = x[1, 1024 * r:1024 * (r + 1), :].T
        out.append(np.ascontiguousarray(np.concatenate([a, b], axis=1), dtype=np.float32))
    return out


def _unshard_x(shards):
    x = np.empty((2, 8192, 2048), dtype=np.float32)
    for r in range(NCORE):
        x[0, 1024 * r:1024 * (r + 1), :] = shards[r][:, 0:1024].T
        x[1, 1024 * r:1024 * (r + 1), :] = shards[r][:, 1024:2048].T
    return x


def run_ffn0(xshards, inp, li=0):
    nc = build_ffn0()
    common = {
        "gwT": np.ascontiguousarray(np.asarray(inp["L0_ffn_gate_w"]).T),
        "WG": np.ascontiguousarray(inp["L0_ffn_we_gate"]),
        "WU": np.ascontiguousarray(inp["L0_ffn_we_up"]),
        "WD": np.ascontiguousarray(inp["L0_ffn_we_down"]),
        "SG": np.ascontiguousarray(inp["L0_ffn_ws_gate"]),
        "SU": np.ascontiguousarray(inp["L0_ffn_ws_up"]),
        "SD": np.ascontiguousarray(inp["L0_ffn_ws_down"]),
    }
    in_maps = [dict(common, xres=xshards[r]) for r in range(NCORE)]
    res = run_bass_kernel_spmd(nc, in_maps, core_ids=list(range(NCORE)))
    return [res.results[r]["xout"] for r in range(NCORE)]




def build_ffn1():
    nc = bass.Bass("TRN2", target_bir_lowering=False)
    stack = ExitStack()
    P = Prog(nc, stack)
    T = TPC
    H = 8192
    NHC = H // 128

    xres = nc.dram_tensor("xres", [D, T], F32, kind="ExternalInput").ap()
    wupT = nc.dram_tensor("wupT", [D, H], I32, kind="ExternalInput").ap()
    scup = nc.dram_tensor("scup", [2, NDC * H], F32, kind="ExternalInput").ap()
    bup = nc.dram_tensor("bup", [128, 64], F32, kind="ExternalInput").ap()
    Aup = nc.dram_tensor("Aup", [D, 16], F32, kind="ExternalInput").ap()
    Bup = nc.dram_tensor("Bup", [16, H], F32, kind="ExternalInput").ap()
    wdnT = nc.dram_tensor("wdnT", [H, D], I32, kind="ExternalInput").ap()
    scdn = nc.dram_tensor("scdn", [2, NHC * D], F32, kind="ExternalInput").ap()
    bdn = nc.dram_tensor("bdn", [128, 16], F32, kind="ExternalInput").ap()
    Adn = nc.dram_tensor("Adn", [H, 16], F32, kind="ExternalInput").ap()
    Bdn = nc.dram_tensor("Bdn", [16, D], F32, kind="ExternalInput").ap()
    xout = nc.dram_tensor("xout", [D, T], F32, kind="ExternalOutput").ap()
    wupbf = nc.dram_tensor("wupbf", [D, H], BF16, kind="Internal").ap()
    wdnbf = nc.dram_tensor("wdnbf", [NDC, NHC, 128, 128], BF16, kind="Internal").ap()

    with stack:
        cx = Ctx(P, stack)
        sb = {
            "ln_sq": stack.enter_context(nc.sbuf_tensor("ln_sq", [128, 512], BF16)),
            "ln_stats": stack.enter_context(nc.sbuf_tensor("ln_stats", [1, 5 * 512], F32)),
            "ln_bcast": stack.enter_context(nc.sbuf_tensor("ln_bcast", [128, 1024], F32)),
        }
        xbf = stack.enter_context(nc.sbuf_tensor("xbf", [128, NDC * 1024], BF16))
        pat2 = stack.enter_context(nc.sbuf_tensor("pat2", [2, 128], BF16))
        sct = stack.enter_context(nc.sbuf_tensor("sct", [2, 1024], BF16))
        qf = [stack.enter_context(nc.sbuf_tensor(f"qf{i}", [128, 512], F32)) for i in range(2)]
        scb = stack.enter_context(nc.sbuf_tensor("scb", [128, 512], F32))
        wq = [stack.enter_context(nc.sbuf_tensor(f"wq{i}", [128, 512], BF16)) for i in range(2)]
        y2acc = stack.enter_context(nc.sbuf_tensor("y2acc", [128, NDC * 1024], F32))
        x2s = stack.enter_context(nc.sbuf_tensor("x2s", [128, 16 * 1024], BF16))
        wst = [stack.enter_context(nc.sbuf_tensor(f"wst{i}", [128, NDC * 128], BF16))
               for i in range(2)]
        wdst = [stack.enter_context(nc.sbuf_tensor(f"wdst{i}", [128, 16 * 128], BF16))
                for i in range(2)]
        aup_sb = stack.enter_context(nc.sbuf_tensor("aup_sb", [128, NDC * 16], BF16))
        adn_sb = stack.enter_context(nc.sbuf_tensor("adn_sb", [128, NHC * 16], BF16))
        bup_sb = stack.enter_context(nc.sbuf_tensor("bup_sb", [128, NHC], F32))
        bupc = stack.enter_context(nc.sbuf_tensor("bupc", [16, 2048], BF16))
        bdnc = stack.enter_context(nc.sbuf_tensor("bdnc", [16, D], BF16))
        bdn_sb = stack.enter_context(nc.sbuf_tensor("bdn_sb", [128, NDC], F32))
        bdnh = stack.enter_context(nc.sbuf_tensor("bdnh", [128, NDC], F32))
        xa_up = stack.enter_context(nc.sbuf_tensor("xa_up", [16, 1024], BF16))
        xa_dn = stack.enter_context(nc.sbuf_tensor("xa_dn", [16, 1024], BF16))
        xold = stack.enter_context(nc.sbuf_tensor("xoldf", [128, 512], F32))
        yrow = stack.enter_context(nc.sbuf_tensor("yrow", [128, 512], F32))

        P.add("gpsimd", lambda eng: eng.memset(pat2[:, :], 1.0), tag="p2a")
        P.add("gpsimd", lambda eng: eng.affine_select(
            out=pat2[:, :], in_=pat2[:, :], compare_op=OP.is_ge, fill=0.0, base=0,
            pattern=[[1, 128]], channel_multiplier=-64), tag="p2b")
        pat_ev = P.add("gpsimd", lambda eng: eng.affine_select(
            out=pat2[:, :], in_=pat2[:, :], compare_op=OP.is_ge, fill=0.0, base=63,
            pattern=[[-1, 128]], channel_multiplier=64), tag="p2c")
        aup_ev = P.dma("gpsimd", aup_sb.rearrange("p (c r) -> p c r", r=16),
                       Aup.rearrange("(c p) r -> p c r", p=128), sem="ldca", tag="aup")
        adn_ev = P.dma("gpsimd", adn_sb.rearrange("p (c r) -> p c r", r=16),
                       Adn.rearrange("(c p) r -> p c r", p=128), sem="ldcb", tag="adn")
        bdnc_ev = P.dma("gpsimd", bdnc[:, :], Bdn[:, :], sem="ldcc", tag="bdnc")
        ld_const = bdnc_ev
        bup_ev = P.dma("sync", bup_sb[:, :], bup[:, :], sem="ldc2", tag="bup")
        bdn_ev = P.dma("sync", bdn_sb[:, :], bdn[:, :], sem="ldc3", tag="bdn")
        bdnh_ev = P.add("vector", lambda eng: eng.tensor_scalar(
            out=bdnh[:, :], in0=bdn_sb[:, :], scalar1=0.5, scalar2=None, op0=OP.mult),
            waits=[bdn_ev, bup_ev], tag="bdnh")

        pbS = cx.pb[3]

        def dequant(srcT, scrows, nrows_chunks, ncols, pfx2, arranged, carry=()):
            prev_scb = None
            prev_w = {0: None, 1: None}
            if carry:
                prev_w = {0: carry[0], 1: carry[1]}
            prev_mm = {}
            last_st = {}
            for c in range(nrows_chunks):
                for cb in range(ncols // 512):
                    bsel = (c * (ncols // 512) + cb) % 2
                    ssl = slice(bsel * 512, bsel * 512 + 512)
                    sc_ev = P.dma("gpsimd", sct[:, ssl],
                                  scrows[:, c * ncols + cb * 512: c * ncols + cb * 512 + 512],
                                  sem=f"ldsc{bsel}" + pfx2, waits=[prev_mm.get(bsel)],
                                  tag=pfx2 + f"sc{c}_{cb}")
                    qev = P.dma("gpsimd", qf[bsel][:, :],
                                srcT[c * 128:(c + 1) * 128, cb * 512:(cb + 1) * 512],
                                sem=f"ldq{bsel}" + pfx2, waits=[prev_w.get(bsel)],
                                tag=pfx2 + f"q{c}_{cb}")
                    mev = P.mm(pbS, pat2[0:2, :], sct[0:2, ssl],
                               True, True, waits=[pat_ev, sc_ev, prev_scb],
                               inc=P.sem("dqm" + pfx2), tag=pfx2 + "sm")
                    prev_mm[bsel] = mev
                    cev = P.add("vector", lambda eng: eng.tensor_copy(out=scb[:, :],
                                                                     in_=pbS[:, :]),
                                waits=[mev], tag=pfx2 + "scc")
                    prev_scb = cev
                    dev = P.add("vector", lambda eng, bsel=bsel: eng.scalar_tensor_tensor(
                        out=wq[bsel][:, :], in0=qf[bsel][:, :], scalar=7.5, in1=scb[:, :],
                        op0=OP.subtract, op1=OP.mult),
                        waits=[qev, cev, prev_w.get(bsel)], tag=pfx2 + "dq")
                    if arranged is None:
                        stev = P.dma("gpsimd",
                                     wupbf[c * 128:(c + 1) * 128, cb * 512:(cb + 1) * 512],
                                     wq[bsel][:, :], sem=f"stq{bsel}" + pfx2, waits=[dev],
                                     tag=pfx2 + f"st{c}_{cb}")
                    else:
                        fc0 = (cb * 512) // 128
                        stev = P.dma("gpsimd",
                                     wdnbf[fc0:fc0 + 4, c, :, :].rearrange("fc p f -> p fc f"),
                                     wq[bsel].rearrange("p (fc f) -> p fc f", f=128),
                                     sem=f"stq{bsel}" + pfx2, waits=[dev],
                                     tag=pfx2 + f"st{c}_{cb}")
                    prev_w[bsel] = stev
                    last_st[bsel] = stev
            return [last_st.get(0), last_st.get(1)]

        up_done = dequant(wupT, scup, NDC, H, "u", None)
        dn_done = dequant(wdnT, scdn, NHC, D, "d", True, carry=up_done)

        pbY1 = [cx.pb[0], cx.pb[1]]
        pbY2 = [cx.pb[4], cx.pb[5]]
        pbXA = cx.pb[2]
        pbB = cx.pb[6]
        NSUP = 4
        prev_wst_cons = {}
        prev_wdst_cons = {}
        prev_y1read = None
        prev_y2read = {}
        prev_xa_rd = None
        prev_res = None
        last_y1_mm = None
        for half in range(2):
            toff = half * 1024
            ln_ev = emit_ln(P, cx, xres[:, toff:toff + 1024], xbf, sb, f"ln{half}",
                            T=1024, first_waits=[last_y1_mm, prev_y1read])
            for tb in range(2):
                mev = None
                for c in range(NDC):
                    mev = P.mm(pbXA[0:16, :], aup_sb[:, c * 16:(c + 1) * 16],
                               xbf[:, c * 1024 + tb * 512: c * 1024 + tb * 512 + 512],
                               start=(c == 0), stop=(c == NDC - 1),
                               waits=[aup_ev, ln_ev, prev_xa_rd] if c == 0 else [],
                               inc=P.sem("xam") if c == NDC - 1 else None, tag="xau")
                prev_xa_rd = P.add("vector", lambda eng, tb=tb: eng.tensor_copy(
                    out=xa_up[:, tb * 512:(tb + 1) * 512], in_=pbXA[0:16, :]),
                    waits=[mev], tag="xauc")
            xaup_ev = prev_xa_rd

            for hs in range(NSUP):
                bupc_ev = P.dma("gpsimd", bupc[:, :], Bup[:, hs * 2048:(hs + 1) * 2048],
                                sem="ldbu", waits=[last_y1_mm], tag=f"bu{half}_{hs}")
                ld_const2 = bupc_ev
                for hc in range(16):
                    hcg = hs * 16 + hc
                    bsel = hcg % 2
                    wev = P.dma("sync", wst[bsel].rearrange("p (c h) -> p c h", h=128),
                                wupbf.rearrange("(c p) h -> p c h", p=128)[:, :, hcg * 128:(hcg + 1) * 128],
                                sem=f"lwu{bsel}",
                                waits=up_done + [prev_wst_cons.get(bsel)],
                                tag=f"lw{half}_{hcg}")
                    for tb in range(2):
                        bi = (hc * 2 + tb) % 2
                        for c in range(NDC):
                            w = [wev, prev_y1read] if c == 0 else []
                            P.mm(pbY1[bi], wst[bsel][:, c * 128:(c + 1) * 128],
                                 xbf[:, c * 1024 + tb * 512: c * 1024 + tb * 512 + 512],
                                 start=(c == 0), stop=False, waits=w, tag="y1m")
                        mev = P.mm(pbY1[bi], bupc[0:16, hc * 128:(hc + 1) * 128],
                                   xa_up[0:16, tb * 512:(tb + 1) * 512], start=False, stop=True,
                                   waits=[ld_const2, xaup_ev], inc=P.sem("y1m"), tag="y1l")
                        prev_wst_cons[bsel] = mev
                        last_y1_mm = mev
                        prev_y1read = P.add("scalar", lambda eng, bi=bi, hc=hc, tb=tb, hcg=hcg:
                            eng.activation(
                                out=x2s[:, hc * 1024 + tb * 512: hc * 1024 + tb * 512 + 512],
                                in_=pbY1[bi][:, :], func=AF.Relu,
                                bias=bup_sb[:, hcg:hcg + 1]),
                            waits=[mev, bup_ev], tag="relu")
                x2_ev = prev_y1read

                for fc in range(NDC):
                    bsel = fc % 2
                    wev = P.dma("sync", wdst[bsel].rearrange("p (hc f) -> p hc f", f=128),
                                wdnbf[fc, hs * 16:(hs + 1) * 16, :, :].rearrange("hc p f -> p hc f"),
                                sem=f"lwd{bsel}",
                                waits=dn_done + [prev_wdst_cons.get(bsel)],
                                tag=f"ld{half}_{hs}_{fc}")
                    for tb in range(2):
                        bi = (fc * 2 + tb) % 2
                        mev = None
                        for hc in range(16):
                            w = [wev, x2_ev, prev_y2read.get(bi)] if hc == 0 else []
                            mev = P.mm(pbY2[bi], wdst[bsel][:, hc * 128:(hc + 1) * 128],
                                       x2s[:, hc * 1024 + tb * 512: hc * 1024 + tb * 512 + 512],
                                       start=(hc == 0), stop=(hc == 15), waits=w,
                                       inc=P.sem("y2m") if hc == 15 else None, tag="y2m")
                        prev_wdst_cons[bsel] = mev
                        osl = slice(fc * 1024 + tb * 512, fc * 1024 + tb * 512 + 512)
                        if hs == 0:
                            prev_y2read[bi] = P.add("vector", lambda eng, osl=osl, bi=bi:
                                eng.tensor_copy(out=y2acc[:, osl], in_=pbY2[bi][:, :]),
                                waits=[mev], tag="y2c")
                        else:
                            prev_y2read[bi] = P.add("vector", lambda eng, osl=osl, bi=bi:
                                eng.tensor_tensor(out=y2acc[:, osl], in0=y2acc[:, osl],
                                                  in1=pbY2[bi][:, :], op=OP.add),
                                waits=[mev], tag="y2a")
                for tb in range(2):
                    bank = pbXA if tb == 0 else pbB
                    for hc in range(16):
                        hcg = hs * 16 + hc
                        w = []
                        if hs == 0 and hc == 0:
                            w = [adn_ev, x2_ev, prev_xa_rd if tb == 0 else prev_res]
                        elif hc == 0:
                            w = [x2_ev]
                        xadn_mev = P.mm(bank[0:16, :], adn_sb[:, hcg * 16:(hcg + 1) * 16],
                                        x2s[:, hc * 1024 + tb * 512: hc * 1024 + tb * 512 + 512],
                                        start=(hs == 0 and hc == 0),
                                        stop=(hs == NSUP - 1 and hc == 15),
                                        waits=w,
                                        inc=P.sem("xam") if (hs == NSUP - 1 and hc == 15) else None,
                                        tag="xad")
                        if tb == 1 and hs == NSUP - 1 and hc == 15:
                            xadn_done = xadn_mev
            P.add("vector", lambda eng: eng.tensor_copy(
                out=xa_dn[:, 0:512], in_=pbXA[0:16, :]), waits=[xadn_done], tag="xadc0")
            prev_xa_rd = P.add("vector", lambda eng: eng.tensor_copy(
                out=xa_dn[:, 512:1024], in_=pbB[0:16, :]), tag="xadc1")
            xadn_c = prev_xa_rd

            for fc in range(NDC):
                for tb in range(2):
                    tsl = slice(toff + tb * 512, toff + tb * 512 + 512)
                    osl = slice(fc * 1024 + tb * 512, fc * 1024 + tb * 512 + 512)
                    bev = P.mm(pbB, bdnc[0:16, fc * 128:(fc + 1) * 128],
                               xa_dn[0:16, tb * 512:(tb + 1) * 512], True, True,
                               waits=[ld_const, xadn_c, prev_res], inc=P.sem("y2m"), tag="bdm")
                    lev = P.dma("sync", xold[:, :], xres[fc * 128:(fc + 1) * 128, tsl],
                                sem="ldr1", waits=[prev_res], tag="lre")
                    vev = P.add("vector", lambda eng, osl=osl: eng.tensor_tensor(
                        out=yrow[:, :], in0=y2acc[:, osl], in1=pbB[:, :], op=OP.add),
                        waits=[bev], tag="yadd")
                    aev = P.add("scalar", lambda eng, fc=fc: eng.activation(
                        out=yrow[:, :], in_=yrow[:, :], func=AF.Identity, scale=0.5,
                        bias=bdnh[:, fc:fc + 1]), waits=[vev, bdnh_ev], tag="yid")
                    rev = P.add("vector", lambda eng: eng.tensor_tensor(
                        out=xold[:, :], in0=xold[:, :], in1=yrow[:, :], op=OP.add),
                        waits=[lev, aev], tag="xadd")
                    prev_res = P.dma("sync", xout[fc * 128:(fc + 1) * 128, tsl], xold[:, :],
                                     sem="stx", waits=[rev], tag="stx")
        P.add("scalar", lambda eng: eng.nop(), waits=[prev_res], tag="fin")
        P.emit()
    return nc


def _prep_ffn1(inp, li):
    p = f"L{li}_ffn_"
    wup_q = np.asarray(inp[p + "w_up_q"])
    wup_sc = np.asarray(inp[p + "w_up_scale"])
    wdn_q = np.asarray(inp[p + "w_down_q"])
    wdn_sc = np.asarray(inp[p + "w_down_scale"])
    H, Dd = 8192, 2048
    scup = np.empty((2, (Dd // 128) * H), dtype=np.float32)
    for c in range(Dd // 128):
        for b in range(2):
            scup[b, c * H:(c + 1) * H] = wup_sc[:, 2 * c + b]
    scdn = np.empty((2, (H // 128) * Dd), dtype=np.float32)
    for c in range(H // 128):
        for b in range(2):
            scdn[b, c * Dd:(c + 1) * Dd] = wdn_sc[:, 2 * c + b]
    return {
        "wupT": np.ascontiguousarray(wup_q.T),
        "scup": scup,
        "bup": np.ascontiguousarray(np.asarray(inp[p + "b_up"]).reshape(64, 128).T),
        "Aup": np.ascontiguousarray(inp[p + "w_up_lora_a"]),
        "Bup": np.ascontiguousarray(inp[p + "w_up_lora_b"]),
        "wdnT": np.ascontiguousarray(wdn_q.T),
        "scdn": scdn,
        "bdn": np.ascontiguousarray(np.asarray(inp[p + "b_down"]).reshape(16, 128).T),
        "Adn": np.ascontiguousarray(inp[p + "w_down_lora_a"]),
        "Bdn": np.ascontiguousarray(inp[p + "w_down_lora_b"]),
    }


def run_ffn1(xshards, inp, li=1):
    nc = build_ffn1()
    common = _prep_ffn1(inp, li)
    in_maps = [dict(common, xres=xshards[r]) for r in range(NCORE)]
    res = run_bass_kernel_spmd(nc, in_maps, core_ids=list(range(NCORE)))
    return [res.results[r]["xout"] for r in range(NCORE)]




def build_ln_proj(OUT, silu_hi):
    nc = bass.Bass("TRN2", target_bir_lowering=False)
    stack = ExitStack()
    P = Prog(nc, stack)
    T = TPC
    NOC = OUT // 128

    xres = nc.dram_tensor("xres", [D, T], F32, kind="ExternalInput").ap()
    W = nc.dram_tensor("W", [D, OUT], F32, kind="ExternalInput").ap()
    yout = nc.dram_tensor("yout", [OUT, T], BF16, kind="ExternalOutput").ap()

    with stack:
        cx = Ctx(P, stack)
        sb = {
            "ln_sq": stack.enter_context(nc.sbuf_tensor("ln_sq", [128, 512], BF16)),
            "ln_stats": stack.enter_context(nc.sbuf_tensor("ln_stats", [1, 5 * 512], F32)),
            "ln_bcast": stack.enter_context(nc.sbuf_tensor("ln_bcast", [128, 1024], F32)),
        }
        xbf = stack.enter_context(nc.sbuf_tensor("xbf", [128, NDC * T], BF16))
        wst = [stack.enter_context(nc.sbuf_tensor(f"wst{i}", [128, NDC * 128], BF16))
               for i in range(2)]
        ob = [stack.enter_context(nc.sbuf_tensor(f"ob{i}", [128, 512], BF16))
              for i in range(2)]

        ln_ev = emit_ln(P, cx, xres, xbf, sb, "ln")

        pbA = [cx.pb[3], cx.pb[4]]
        prev_w_cons = {}
        prev_read = {}
        prev_store = {}
        for oc in range(NOC):
            bsel = oc % 2
            wev = P.dma("gpsimd", wst[bsel].rearrange("p (c o) -> p c o", o=128),
                        W.rearrange("(c p) o -> p c o", p=128)[:, :, oc * 128:(oc + 1) * 128],
                        sem=f"lw{bsel}", waits=[prev_w_cons.get(bsel)], tag=f"w{oc}")
            for tb in range(4):
                bi = (oc * 4 + tb) % 2
                mev = None
                for c in range(NDC):
                    w = [wev, ln_ev, prev_read.get(bi)] if c == 0 else []
                    mev = P.mm(pbA[bi], wst[bsel][:, c * 128:(c + 1) * 128],
                               xbf[:, c * T + tb * 512: c * T + tb * 512 + 512],
                               start=(c == 0), stop=(c == NDC - 1), waits=w,
                               inc=P.sem("pm") if c == NDC - 1 else None, tag="pm")
                prev_w_cons[bsel] = mev
                if oc * 128 < silu_hi:
                    prev_read[bi] = P.add("scalar", lambda eng, bi=bi: eng.activation(
                        out=ob[bi][:, :], in_=pbA[bi][:, :], func=AF.Silu),
                        waits=[mev, prev_store.get(bi)], tag="sl")
                else:
                    prev_read[bi] = P.add("vector", lambda eng, bi=bi: eng.tensor_copy(
                        out=ob[bi][:, :], in_=pbA[bi][:, :]),
                        waits=[mev, prev_store.get(bi)], tag="cp")
                prev_store[bi] = P.dma("sync",
                                       yout[oc * 128:(oc + 1) * 128, tb * 512:(tb + 1) * 512],
                                       ob[bi][:, :], sem=f"st{bi}", waits=[prev_read[bi]],
                                       tag=f"s{oc}_{tb}")
        P.add("scalar", lambda eng: eng.nop(),
              waits=[prev_store[0], prev_store[1]], tag="fin")
        P.emit()
    return nc


def build_att0_scan():
    nc = bass.Bass("TRN2", target_bir_lowering=False)
    stack = ExitStack()
    P = Prog(nc, stack)
    S = 8192
    NB = 32
    NTC = S // 128

    qh = nc.dram_tensor("qh", [4, 128, S], BF16, kind="ExternalInput").ap()
    kh = nc.dram_tensor("kh", [4, 128, S], BF16, kind="ExternalInput").ap()
    kth = nc.dram_tensor("kth", [4, S, 128], BF16, kind="ExternalInput").ap()
    vth = nc.dram_tensor("vth", [4, S, 128], BF16, kind="ExternalInput").ap()
    slopes = nc.dram_tensor("slopes", [1, 4], F32, kind="ExternalInput").ap()
    oh = nc.dram_tensor("oh", [4, 128, S], BF16, kind="ExternalOutput").ap()

    with stack:
        cx = Ctx(P, stack)
        qsb = [stack.enter_context(nc.sbuf_tensor(f"qsb{i}", [128, S], BF16)) for i in range(2)]
        ksb = [stack.enter_context(nc.sbuf_tensor(f"ksb{i}", [128, S], BF16)) for i in range(2)]
        ktsb = [stack.enter_context(nc.sbuf_tensor(f"ktsb{i}", [128, S], BF16))
                for i in range(2)]
        vtsb = [stack.enter_context(nc.sbuf_tensor(f"vtsb{i}", [128, S], BF16))
                for i in range(2)]
        osb = [stack.enter_context(nc.sbuf_tensor(f"osb{i}", [128, S], BF16)) for i in range(2)]
        slp = stack.enter_context(nc.sbuf_tensor("slp", [1, 4], F32))
        scl = stack.enter_context(nc.sbuf_tensor("scl", [1, 8], F32))
        iot = stack.enter_context(nc.sbuf_tensor("iot", [128, 256], F32))
        qdrow = stack.enter_context(nc.sbuf_tensor("qdrow", [1, 256], F32))
        dgf = stack.enter_context(nc.sbuf_tensor("dgf", [128, 256], F32))
        qdec = stack.enter_context(nc.sbuf_tensor("qdec", [128, 4 * 256], BF16))
        kdec = stack.enter_context(nc.sbuf_tensor("kdec", [128, 4 * 2], F32))
        dgt = stack.enter_context(nc.sbuf_tensor("dgt", [128, 4 * 512], BF16))
        bdc = stack.enter_context(nc.sbuf_tensor("bdc", [128, 4], F32))
        kv32 = stack.enter_context(nc.sbuf_tensor("kv32", [128, 128], F32))
        kvbf = stack.enter_context(nc.sbuf_tensor("kvbf", [128, 128], BF16))
        qd = stack.enter_context(nc.sbuf_tensor("qd", [128, 256], BF16))
        qkm = stack.enter_context(nc.sbuf_tensor("qkm", [128, 2 * 256], BF16))
        ktl = stack.enter_context(nc.sbuf_tensor("ktl", [128, 2 * 128], BF16))
        negs = stack.enter_context(nc.sbuf_tensor("negs", [128, 1], F32))

        pbO = cx.pb[0]
        pbQK = [cx.pb[1], cx.pb[2]]
        pbKV = cx.pb[3]
        pbB = cx.pb[4]

        sl_ev = P.dma("sync", slp[:, :], slopes[:, :], sem="lds", tag="slp")

        prev_bev = None; prev_bev2 = None; prev_qdcp = None
        prev_ngev = None; prev_bdcev = None
        prev_iot_rd = None; prev_negs_rd = None
        for p in range(4):
            nev = P.add("vector", lambda eng, p=p: eng.tensor_scalar(
                out=scl[:, 0:1], in0=slp[:, p:p + 1], scalar1=-1.0, scalar2=None,
                op0=OP.mult), waits=[sl_ev, prev_bev], tag=f"ns{p}")
            bev = P.mm(pbB[:, 0:1], cx.ones_row32, scl[:, 0:1], True, True,
                       waits=[nev, cx.const_ev, prev_ngev, prev_qdcp], inc=P.sem("cm"),
                       tag=f"nb{p}")
            prev_bev = bev
            ngev = P.add("vector", lambda eng: eng.tensor_copy(out=negs[:, :], in_=pbB[:, 0:1]),
                         waits=[bev, prev_negs_rd], tag=f"nc{p}")
            prev_ngev = ngev
            bdev = P.add("scalar", lambda eng, p=p: eng.activation(
                out=scl[:, 1:2], in_=slp[:, p:p + 1], func=AF.Exp, scale=-256.0),
                waits=[sl_ev, prev_bev2], tag=f"bd{p}")
            bev2 = P.mm(pbB[:, 1:2], cx.ones_row32, scl[:, 1:2], True, True,
                        waits=[bdev, ngev, prev_bdcev], inc=P.sem("cm"), tag=f"bb{p}")
            prev_bev2 = bev2
            prev_bdcev = P.add("vector", lambda eng, p=p: eng.tensor_copy(
                out=bdc[:, p:p + 1], in_=pbB[:, 1:2]), waits=[bev2], tag=f"bc{p}")
            i1 = P.add("gpsimd", lambda eng: eng.iota(
                out=iot[0:1, 0:256], pattern=[[1, 256]], base=1, channel_multiplier=0,
                allow_small_or_imprecise_dtypes=True), waits=[prev_iot_rd], tag=f"i1{p}")
            qrev = P.add("scalar", lambda eng: eng.activation(
                out=qdrow[:, :], in_=iot[0:1, 0:256], func=AF.Exp, scale=negs[0:1, 0:1]),
                waits=[i1, ngev], tag=f"qr{p}")
            bev3 = P.mm(pbB[:, 0:256], cx.ones_row32, qdrow[:, :], True, True,
                        waits=[qrev, prev_qdcp, prev_bdcev], inc=P.sem("cm"), tag=f"qb{p}")
            prev_qdcp = P.add("vector", lambda eng, p=p: eng.tensor_copy(
                out=qdec[:, p * 256:(p + 1) * 256], in_=pbB[:, 0:256]),
                waits=[bev3], tag=f"qd{p}")
            i2 = P.add("gpsimd", lambda eng: eng.iota(
                out=iot[:, 0:2], pattern=[[-128, 2]], base=255, channel_multiplier=-1,
                allow_small_or_imprecise_dtypes=True), waits=[qrev], tag=f"i2{p}")
            kdev = P.add("scalar", lambda eng, p=p: eng.activation(
                out=kdec[:, p * 2:(p + 1) * 2], in_=iot[:, 0:2], func=AF.Exp,
                scale=negs[:, 0:1]), waits=[i2], tag=f"kd{p}")
            prev_iot_rd = kdev
            prev_dgcp = None
            for j in range(2):
                i3 = P.add("gpsimd", lambda eng, j=j: eng.iota(
                    out=iot[:, 0:256], pattern=[[1, 256]], base=-128 * j,
                    channel_multiplier=-1, allow_small_or_imprecise_dtypes=True),
                    waits=[prev_iot_rd], tag=f"i3{p}_{j}")
                dev = P.add("scalar", lambda eng, j=j: eng.activation(
                    out=dgf[:, :], in_=iot[:, 0:256], func=AF.Exp, scale=negs[:, 0:1]),
                    waits=[i3, prev_dgcp], tag=f"dg{p}_{j}")
                prev_iot_rd = dev
                prev_negs_rd = dev
                mev = P.add("gpsimd", lambda eng, j=j: eng.affine_select(
                    out=dgf[:, :], in_=dgf[:, :], compare_op=OP.is_ge, fill=0.0,
                    base=-128 * j, pattern=[[1, 256]], channel_multiplier=-1),
                    waits=[dev], tag=f"dm{p}_{j}")
                prev_dgcp = P.add("vector", lambda eng, p=p, j=j: eng.tensor_copy(
                    out=dgt[:, p * 512 + j * 256: p * 512 + j * 256 + 256], in_=dgf[:, :]),
                    waits=[mev], tag=f"dc{p}_{j}")
        const_done = P.vev()

        def pload(p, prev_done):
            bsel = p % 2
            evs = [
                P.dma("gpsimd", qsb[bsel][:, :], qh[p], sem=f"lq{bsel}",
                      waits=prev_done, tag=f"lq{p}"),
                P.dma("gpsimd", ksb[bsel][:, :], kh[p], sem=f"lk{bsel}",
                      waits=prev_done, tag=f"lk{p}"),
                P.dma("gpsimd", ktsb[bsel].rearrange("p (tc d) -> p tc d", d=128),
                      kth[p].rearrange("(tc p) d -> p tc d", p=128),
                      sem=f"lkt{bsel}", waits=prev_done, tag=f"lkt{p}"),
                P.dma("gpsimd", vtsb[bsel].rearrange("p (tc d) -> p tc d", d=128),
                      vth[p].rearrange("(tc p) d -> p tc d", p=128),
                      sem=f"lvt{bsel}", waits=prev_done, tag=f"lvt{p}"),
            ]
            return evs

        pair_done = {}
        pl_evs = {0: pload(0, []), 1: pload(1, [])}
        for p in range(4):
            bsel = p % 2
            Q, K, KT, VT, O = qsb[bsel], ksb[bsel], ktsb[bsel], vtsb[bsel], osb[bsel]
            ld_evs = pl_evs[p]
            if p + 2 < 4:
                pass

            kvm_ev = P.add("vector", lambda eng: eng.memset(kv32[:, :], 0.0), tag=f"kz{p}")
            kvb_ev = P.add("vector", lambda eng: eng.memset(kvbf[:, :], 0.0), tag=f"kzb{p}")

            prev_nd = None
            prev_evac = None
            prev_qkm = [None, None]
            prev_diag_stop = None
            prev_kv_stop = None
            prev_stt = None
            for i in range(NB):
                t0 = i * 256
                qdev = P.add("vector", lambda eng, t0=t0, p=p, Q=Q: eng.tensor_tensor(
                    out=qd[:, :], in0=Q[:, t0:t0 + 256],
                    in1=qdec[:, p * 256:(p + 1) * 256], op=OP.mult),
                    waits=[ld_evs[0], const_done, prev_nd], tag=f"qd{p}_{i}")
                nd_ev = P.mm(pbO[:, 0:256], kvbf, qd, start=True, stop=False,
                             waits=[qdev, kvb_ev, prev_evac], inc=P.sem("om"), tag=f"nd{p}")
                prev_nd = nd_ev
                qk_evs = []
                for j in range(2):
                    qk_evs.append(P.mm(pbQK[j][:, 0:256], K[:, t0 + j * 128:t0 + j * 128 + 128],
                                       Q[:, t0:t0 + 256], start=True, stop=True,
                                       waits=[ld_evs[1], prev_qkm[j]],
                                       inc=P.sem("qkm"), tag=f"qk{p}_{j}"))
                for j in range(2):
                    prev_qkm[j] = P.add("vector", lambda eng, j=j, p=p: eng.tensor_tensor(
                        out=qkm[:, j * 256:(j + 1) * 256], in0=pbQK[j][:, 0:256],
                        in1=dgt[:, p * 512 + j * 256: p * 512 + j * 256 + 256], op=OP.mult),
                        waits=[qk_evs[j]], tag=f"qm{p}_{j}")
                dstop = None
                for j in range(2):
                    dstop = P.mm(pbO[:, 0:256], VT[:, t0 + j * 128:t0 + j * 128 + 128],
                                 qkm[:, j * 256:(j + 1) * 256], start=False, stop=(j == 1),
                                 waits=[ld_evs[3], prev_qkm[j]],
                                 inc=P.sem("om") if j == 1 else None, tag=f"dg{p}_{j}")
                prev_evac = P.add("vector", lambda eng, t0=t0, O=O: eng.tensor_copy(
                    out=O[:, t0:t0 + 256], in_=pbO[:, 0:256]), waits=[dstop], tag=f"ev{p}")
                kt_evs = []
                for j in range(2):
                    kt_evs.append(P.add("vector",
                        lambda eng, t0=t0, j=j, p=p, KT=KT: eng.tensor_scalar(
                        out=ktl[:, j * 128:(j + 1) * 128],
                        in0=KT[:, t0 + j * 128:t0 + j * 128 + 128],
                        scalar1=kdec[:, p * 2 + j:p * 2 + j + 1], scalar2=None, op0=OP.mult),
                        waits=[ld_evs[2], const_done, prev_kv_stop], tag=f"kt{p}_{j}"))
                for j in range(2):
                    kvst = P.mm(pbKV[:, 0:128], ktl[:, j * 128:(j + 1) * 128],
                                VT[:, t0 + j * 128:t0 + j * 128 + 128],
                                start=(j == 0), stop=(j == 1),
                                waits=[kt_evs[j]] + ([prev_stt] if j == 0 else []),
                                inc=P.sem("kvm") if j == 1 else None, tag=f"kv{p}_{j}")
                prev_kv_stop = kvst
                prev_stt = P.add("vector", lambda eng, p=p: eng.scalar_tensor_tensor(
                    out=kv32[:, :], in0=kv32[:, :], scalar=bdc[:, p:p + 1],
                    in1=pbKV[:, 0:128], op0=OP.mult, op1=OP.add),
                    waits=[kvst, kvm_ev], tag=f"st{p}")
                kvb_ev = P.add("vector", lambda eng: eng.tensor_copy(
                    out=kvbf[:, :], in_=kv32[:, :]), waits=[dstop], tag=f"kb{p}")
            st_ev = P.dma("gpsimd", oh[p], O[:, :], sem=f"soh{bsel}",
                          waits=[prev_evac], tag=f"soh{p}")
            pair_done[p] = [st_ev, prev_stt, prev_kv_stop]
            if p + 2 < 4:
                pl_evs[p + 2] = pload(p + 2, pair_done[p])
        P.add("scalar", lambda eng: eng.nop(),
              waits=pair_done[2] + pair_done[3], tag="fin")
        P.emit()
    return nc




def build_att0_post():
    nc = bass.Bass("TRN2", target_bir_lowering=False)
    stack = ExitStack()
    P = Prog(nc, stack)
    T = TPC

    xres = nc.dram_tensor("xres", [D, T], F32, kind="ExternalInput").ap()
    ofm = nc.dram_tensor("ofm", [D, T], BF16, kind="ExternalInput").ap()
    ufm = nc.dram_tensor("ufm", [D, T], BF16, kind="ExternalInput").ap()
    WoT = nc.dram_tensor("WoT", [D, D], F32, kind="ExternalInput").ap()
    xout = nc.dram_tensor("xout", [D, T], F32, kind="ExternalOutput").ap()

    with stack:
        cx = Ctx(P, stack)
        osb = stack.enter_context(nc.sbuf_tensor("osb", [128, NDC * T], BF16))
        usb = stack.enter_context(nc.sbuf_tensor("usb", [128, NDC * T], BF16))
        sq = stack.enter_context(nc.sbuf_tensor("sq", [128, 512], BF16))
        rows = stack.enter_context(nc.sbuf_tensor("rows", [1, 512], F32))
        rbc = stack.enter_context(nc.sbuf_tensor("rbc", [128, 512], F32))
        wst = [stack.enter_context(nc.sbuf_tensor(f"wst{i}", [128, NDC * 128], BF16))
               for i in range(2)]
        xold = stack.enter_context(nc.sbuf_tensor("xold", [128, 512], F32))

        pbS = cx.pb[0]
        pbB = cx.pb[1]
        pbY = [cx.pb[2], cx.pb[3]]

        o_ev = None
        u_ev = None
        for c in range(NDC):
            o_ev = P.dma("sync", osb[:, c * T:(c + 1) * T], ofm[c * 128:(c + 1) * 128, :],
                         sem="ldo", tag=f"lo{c}")
            u_ev = P.dma("sync", usb[:, c * T:(c + 1) * T], ufm[c * 128:(c + 1) * 128, :],
                         sem="ldu", tag=f"lu{c}")

        prev_rd = None
        prev_bb = None
        gate_evs = []
        for tb in range(4):
            mm_ev = None
            prev_sq = None
            for c in range(NDC):
                csl = slice(c * T + tb * 512, c * T + tb * 512 + 512)
                sev = P.add("scalar", lambda eng, csl=csl: eng.activation(
                    out=sq[:, :], in_=osb[:, csl], func=AF.Square),
                    waits=[o_ev, prev_sq], tag="sq")
                w = [sev]
                if c == 0 and prev_rd is not None:
                    w.append(prev_rd)
                mm_ev = prev_sq = P.mm(pbS[0:1, :], cx.ones_col, sq[:, :],
                                       start=(c == 0), stop=(c == NDC - 1),
                                       waits=w + [cx.const_ev], inc=P.sem("sm"), tag="sm")
            aev = P.add("scalar", lambda eng: eng.activation(
                out=rows[:, :], in_=pbS[0:1, 0:512], func=AF.Ln, scale=1.0 / D,
                bias=cx.consts[0:1, 1:2]), waits=[mm_ev, cx.const_ev], tag="ln")
            prev_rd = aev
            aev = P.add("scalar", lambda eng: eng.activation(
                out=rows[:, :], in_=rows[:, :], func=AF.Exp, scale=-0.5), tag="ex")
            bev = P.mm(pbB, cx.ones_row32, rows[:, :], True, True,
                       waits=[aev, prev_bb], inc=P.sem("sm"), tag="bb")
            cev = P.add("vector", lambda eng: eng.tensor_copy(out=rbc[:, :], in_=pbB[:, :]),
                        waits=[bev], tag="cb")
            prev_bb = cev
            for c in range(NDC):
                csl = slice(c * T + tb * 512, c * T + tb * 512 + 512)
                P.add("vector", lambda eng, csl=csl: eng.tensor_tensor(
                    out=osb[:, csl], in0=osb[:, csl], in1=rbc[:, :], op=OP.mult), tag="g1")
                gev = P.add("vector", lambda eng, csl=csl: eng.tensor_tensor(
                    out=osb[:, csl], in0=osb[:, csl], in1=usb[:, csl], op=OP.mult),
                    waits=[u_ev], tag="g2")
            gate_evs.append(gev)
        gate_done = gate_evs[-1]

        prev_w_cons = {}
        prev_read = {}
        prev_store = None
        for fc in range(NDC):
            bsel = fc % 2
            wev = P.dma("gpsimd", wst[bsel].rearrange("p (c o) -> p c o", o=128),
                        WoT.rearrange("(c p) o -> p c o", p=128)[:, :, fc * 128:(fc + 1) * 128],
                        sem=f"lw{bsel}", waits=[prev_w_cons.get(bsel)], tag=f"w{fc}")
            for tb in range(4):
                bi = (fc * 4 + tb) % 2
                mev = None
                for c in range(NDC):
                    w = [wev, gate_done, prev_read.get(bi)] if c == 0 else []
                    mev = P.mm(pbY[bi], wst[bsel][:, c * 128:(c + 1) * 128],
                               osb[:, c * T + tb * 512: c * T + tb * 512 + 512],
                               start=(c == 0), stop=(c == NDC - 1), waits=w,
                               inc=P.sem("pm") if c == NDC - 1 else None, tag="pm")
                prev_w_cons[bsel] = mev
                lev = P.dma("sync", xold[:, :],
                            xres[fc * 128:(fc + 1) * 128, tb * 512:(tb + 1) * 512],
                            sem="ldr", waits=[prev_store], tag="lr")
                rev = P.add("vector", lambda eng, bi=bi: eng.tensor_tensor(
                    out=xold[:, :], in0=xold[:, :], in1=pbY[bi][:, :], op=OP.add),
                    waits=[lev, mev], tag="rs")
                prev_read[bi] = rev
                prev_store = P.dma("sync",
                                   xout[fc * 128:(fc + 1) * 128, tb * 512:(tb + 1) * 512],
                                   xold[:, :], sem="stx", waits=[rev], tag="sx")
        P.add("scalar", lambda eng: eng.nop(), waits=[prev_store], tag="fin")
        P.emit()
    return nc


def run_att0(xshards, inp):
    W4 = np.ascontiguousarray(np.asarray(inp["L0_attn_W_qkvu"]).T)
    nc = build_ln_proj(8192, silu_hi=4096)
    in_maps = [{"xres": xshards[r], "W": W4} for r in range(NCORE)]
    res = run_bass_kernel_spmd(nc, in_maps, core_ids=list(range(NCORE)))
    qkvu = [res.results[r]["yout"] for r in range(NCORE)]

    slopes_np = np.asarray(inp["L0_attn_slope_rate"]).reshape(16)
    def gather(rows0, rows1, b):
        return np.concatenate([qkvu[c][rows0:rows1, b * 1024:(b + 1) * 1024]
                               for c in range(NCORE)], axis=1)
    in_maps = []
    for r in range(NCORE):
        qh = np.empty((4, 128, 8192), dtype=ml_dtypes.bfloat16)
        kh = np.empty((4, 128, 8192), dtype=ml_dtypes.bfloat16)
        kth = np.empty((4, 8192, 128), dtype=ml_dtypes.bfloat16)
        vth = np.empty((4, 8192, 128), dtype=ml_dtypes.bfloat16)
        slp = np.empty((1, 4), dtype=np.float32)
        for i, (b, h) in enumerate([(b, h) for b in (0, 1) for h in (2 * r, 2 * r + 1)]):
            qh[i] = gather(h * 128, (h + 1) * 128, b)
            kh[i] = gather(2048 + h * 128, 2048 + (h + 1) * 128, b)
            kth[i] = kh[i].T
            vth[i] = gather(4096 + h * 128, 4096 + (h + 1) * 128, b).T
            slp[0, i] = slopes_np[h]
        in_maps.append({"qh": qh, "kh": kh, "kth": np.ascontiguousarray(kth),
                        "vth": np.ascontiguousarray(vth), "slopes": slp})
    nc = build_att0_scan()
    res = run_bass_kernel_spmd(nc, in_maps, core_ids=list(range(NCORE)))
    ohs = [res.results[r]["oh"] for r in range(NCORE)]

    o_full = np.empty((2048, 2, 8192), dtype=ml_dtypes.bfloat16)
    for r in range(NCORE):
        for i, (b, h) in enumerate([(b, h) for b in (0, 1) for h in (2 * r, 2 * r + 1)]):
            o_full[h * 128:(h + 1) * 128, b, :] = ohs[r][i]
    in_maps = []
    WoT = np.ascontiguousarray(np.asarray(inp["L0_attn_W_out"]).T)
    for r in range(NCORE):
        ofm = np.concatenate([o_full[:, 0, r * 1024:(r + 1) * 1024],
                              o_full[:, 1, r * 1024:(r + 1) * 1024]], axis=1)
        ufm = qkvu[r][6144:8192, :]
        in_maps.append({"xres": xshards[r], "ofm": np.ascontiguousarray(ofm),
                        "ufm": np.ascontiguousarray(ufm), "WoT": WoT})
    nc = build_att0_post()
    res = run_bass_kernel_spmd(nc, in_maps, core_ids=list(range(NCORE)))
    return [res.results[r]["xout"] for r in range(NCORE)]




def build_att1_flash():
    nc = bass.Bass("TRN2", target_bir_lowering=False)
    stack = ExitStack()
    P = Prog(nc, stack)
    S = 8192
    NQB = 16
    ISCL = float(1.0 / np.sqrt(128.0))

    qh = nc.dram_tensor("qh", [4, 128, S], BF16, kind="ExternalInput").ap()
    kh = nc.dram_tensor("kh", [4, 128, S], BF16, kind="ExternalInput").ap()
    vth = nc.dram_tensor("vth", [4, S, 128], BF16, kind="ExternalInput").ap()
    ctxh = nc.dram_tensor("ctxh", [4, 128, S], BF16, kind="ExternalOutput").ap()

    with stack:
        cx = Ctx(P, stack)
        qsb = [stack.enter_context(nc.sbuf_tensor(f"qsb{i}", [128, S], BF16)) for i in range(2)]
        ksb = [stack.enter_context(nc.sbuf_tensor(f"ksb{i}", [128, S], BF16)) for i in range(2)]
        vtsb = [stack.enter_context(nc.sbuf_tensor(f"vtsb{i}", [128, S], BF16))
                for i in range(2)]
        csb = [stack.enter_context(nc.sbuf_tensor(f"csb{i}", [128, S], BF16)) for i in range(2)]
        sq = stack.enter_context(nc.sbuf_tensor("sq", [128, 512], BF16))
        nrow = stack.enter_context(nc.sbuf_tensor("nrow", [1, 8], F32))
        masks = stack.enter_context(nc.sbuf_tensor("masks", [128, 4 * 512], F32))
        negM = stack.enter_context(nc.sbuf_tensor("negM", [128, 1], F32))
        eb = [stack.enter_context(nc.sbuf_tensor(f"eb{i}", [128, 512], BF16)) for i in range(2)]
        rrow = stack.enter_context(nc.sbuf_tensor("rrow", [1, 512], F32))
        rb = stack.enter_context(nc.sbuf_tensor("rb", [128, 512], F32))

        pbCtx = cx.pb[0]
        pbS = [cx.pb[1], cx.pb[2]]
        pbL = cx.pb[3]
        pbB = cx.pb[4]
        pbN = cx.pb[5]

        mask_evs = []
        for v in range(4):
            msl = masks[:, v * 512:(v + 1) * 512]
            P.add("gpsimd", lambda eng, msl=msl: eng.memset(msl, 0.0), tag=f"mz{v}")
            mask_evs.append(P.add("gpsimd", lambda eng, msl=msl, v=v: eng.affine_select(
                out=msl, in_=msl, compare_op=OP.is_ge, fill=-1e9, base=-128 * v,
                pattern=[[1, 512]], channel_multiplier=-1), tag=f"ma{v}"))
        mask_done = mask_evs[-1]

        def pload(p, prev_done):
            bsel = p % 2
            return [
                P.dma("gpsimd", qsb[bsel][:, :], qh[p], sem=f"lq{bsel}",
                      waits=prev_done, tag=f"lq{p}"),
                P.dma("gpsimd", ksb[bsel][:, :], kh[p], sem=f"lk{bsel}",
                      waits=prev_done, tag=f"lk{p}"),
                P.dma("gpsimd", vtsb[bsel].rearrange("p (tc d) -> p tc d", d=128),
                      vth[p].rearrange("(tc p) d -> p tc d", p=128),
                      sem=f"lvt{bsel}", waits=prev_done, tag=f"lvt{p}"),
            ]

        pair_done = {}
        pl_evs = {0: pload(0, []), 1: pload(1, [])}
        prev_negM_rd = None
        for p in range(4):
            bsel = p % 2
            Q, K, VT, C = qsb[bsel], ksb[bsel], vtsb[bsel], csb[bsel]
            ld_evs = pl_evs[p]

            prev_nmm = None
            prev_nrd = None
            for t_i, (buf, ld, dst) in enumerate([(Q, ld_evs[0], 0), (K, ld_evs[1], 1)]):
                for tcb in range(16):
                    sev = P.add("scalar", lambda eng, buf=buf, tcb=tcb: eng.activation(
                        out=sq[:, :], in_=buf[:, tcb * 512:(tcb + 1) * 512], func=AF.Square),
                        waits=[ld, prev_nmm], tag=f"nq{p}")
                    prev_nmm = P.mm(pbN[0:1, :], cx.ones_col, sq[:, :], True, True,
                                    waits=[sev, cx.const_ev, prev_nrd], inc=P.sem("nm"),
                                    tag=f"nn{p}")
                    if tcb == 0:
                        prev_nrd = P.add("vector", lambda eng, dst=dst: eng.reduce_max(
                            nrow[:, dst:dst + 1], pbN[0:1, 0:512], axis=AX.X),
                            waits=[prev_nmm], tag=f"nr{p}")
                    else:
                        prev_nrd = P.add("vector", lambda eng: eng.reduce_max(
                            nrow[:, 2:3], pbN[0:1, 0:512], axis=AX.X),
                            waits=[prev_nmm], tag=f"nr{p}")
                        P.add("vector", lambda eng, dst=dst: eng.tensor_tensor(
                            out=nrow[:, dst:dst + 1], in0=nrow[:, dst:dst + 1],
                            in1=nrow[:, 2:3], op=OP.max), tag=f"nx{p}")
            P.add("vector", lambda eng: eng.tensor_tensor(
                out=nrow[:, 3:4], in0=nrow[:, 0:1], in1=nrow[:, 1:2], op=OP.mult),
                tag=f"nm1{p}")
            aev = P.add("scalar", lambda eng: eng.activation(
                out=nrow[:, 3:4], in_=nrow[:, 3:4], func=AF.Ln), waits=[P.vev()],
                tag=f"nl{p}")
            aev = P.add("scalar", lambda eng: eng.activation(
                out=nrow[:, 3:4], in_=nrow[:, 3:4], func=AF.Exp, scale=0.5), tag=f"ne{p}")
            vev = P.add("vector", lambda eng: eng.tensor_scalar(
                out=nrow[:, 3:4], in0=nrow[:, 3:4], scalar1=-ISCL, scalar2=None,
                op0=OP.mult), waits=[aev], tag=f"ns{p}")
            bev = P.mm(pbB[:, 0:1], cx.ones_row32, nrow[:, 3:4], True, True,
                       waits=[vev, prev_negM_rd], inc=P.sem("nm"), tag=f"nb{p}")
            ng_ev = P.add("vector", lambda eng: eng.tensor_copy(out=negM[:, :],
                                                                in_=pbB[:, 0:1]),
                          waits=[bev], tag=f"ng{p}")

            prev_ctx_rd = None
            prev_l_rd = None
            prev_exp = [None, None]
            prev_e_rd = [None, None]
            prev_st = {}
            for qi in range(NQB):
                qsl = slice(qi * 512, qi * 512 + 512)
                nkc = 4 * qi + 4
                lstop = None
                cstop = None
                for kc in range(nkc):
                    sb_i = kc % 2
                    smm = P.mm(pbS[sb_i], K[:, kc * 128:(kc + 1) * 128], Q[:, qsl],
                               start=True, stop=True,
                               waits=[ld_evs[1], ld_evs[0], prev_exp[sb_i],
                                      prev_ctx_rd if kc == 0 else None,
                                      prev_l_rd if kc == 0 else None],
                               inc=P.sem("sm"), tag=f"s{p}_{qi}")
                    o = kc - 4 * qi
                    if o >= 0:
                        mev = P.add("vector", lambda eng, sb_i=sb_i, o=o: eng.tensor_tensor(
                            out=pbS[sb_i][:, :], in0=pbS[sb_i][:, :],
                            in1=masks[:, o * 512:(o + 1) * 512], op=OP.add),
                            waits=[smm, mask_done], tag=f"mk{p}")
                        smm = mev
                    eev = P.add("scalar", lambda eng, sb_i=sb_i: eng.activation(
                        out=eb[sb_i][:, :], in_=pbS[sb_i][:, :], func=AF.Exp, scale=ISCL,
                        bias=negM[:, 0:1]),
                        waits=[smm, ng_ev, prev_e_rd[sb_i]], tag=f"e{p}")
                    prev_exp[sb_i] = eev
                    lstop = P.mm(pbL[0:1, :], cx.ones_col, eb[sb_i][:, :],
                                 start=(kc == 0), stop=(kc == nkc - 1),
                                 waits=[eev], inc=P.sem("lm") if kc == nkc - 1 else None,
                                 tag=f"l{p}")
                    cstop = P.mm(pbCtx, VT[:, kc * 128:(kc + 1) * 128], eb[sb_i][:, :],
                                 start=(kc == 0), stop=(kc == nkc - 1),
                                 waits=[ld_evs[2]],
                                 inc=P.sem("cm2") if kc == nkc - 1 else None, tag=f"c{p}")
                    prev_e_rd[sb_i] = cstop
                aev = P.add("scalar", lambda eng: eng.activation(
                    out=rrow[:, :], in_=pbL[0:1, 0:512], func=AF.Ln), waits=[lstop],
                    tag=f"rl{p}")
                prev_l_rd = aev
                aev = P.add("scalar", lambda eng: eng.activation(
                    out=rrow[:, :], in_=rrow[:, :], func=AF.Exp, scale=-1.0), tag=f"re{p}")
                bev = P.mm(pbB[:, 0:512], cx.ones_row32, rrow[:, :], True, True,
                           waits=[aev], inc=P.sem("nm"), tag=f"rb{p}")
                cev = P.add("vector", lambda eng: eng.tensor_copy(out=rb[:, :],
                                                                 in_=pbB[:, 0:512]),
                            waits=[bev], tag=f"rc{p}")
                tev = P.add("vector", lambda eng, qsl=qsl, C=C: eng.tensor_tensor(
                    out=C[:, qsl], in0=pbCtx[:, :], in1=rb[:, :], op=OP.mult),
                    waits=[cev, cstop, prev_st.get(bsel)], tag=f"ct{p}")
                prev_ctx_rd = tev
            st_ev = P.dma("gpsimd", ctxh[p], C[:, :], sem=f"sc{bsel}",
                          waits=[prev_ctx_rd], tag=f"sc{p}")
            prev_negM_rd = prev_exp[0]
            pair_done[p] = [st_ev, prev_ctx_rd, cstop]
            if p + 2 < 4:
                pl_evs[p + 2] = pload(p + 2, pair_done[p])
        P.add("scalar", lambda eng: eng.nop(), waits=pair_done[2] + pair_done[3], tag="fin")
        P.emit()
    return nc


def build_att1_post():
    nc = bass.Bass("TRN2", target_bir_lowering=False)
    stack = ExitStack()
    P = Prog(nc, stack)
    T = TPC

    xres = nc.dram_tensor("xres", [D, T], F32, kind="ExternalInput").ap()
    cfm = nc.dram_tensor("cfm", [D, T], BF16, kind="ExternalInput").ap()
    Wo = nc.dram_tensor("Wo", [D, D], F32, kind="ExternalInput").ap()
    bo = nc.dram_tensor("bo", [128, 16], F32, kind="ExternalInput").ap()
    xout = nc.dram_tensor("xout", [D, T], F32, kind="ExternalOutput").ap()

    with stack:
        cx = Ctx(P, stack)
        csb = stack.enter_context(nc.sbuf_tensor("csb", [128, NDC * T], BF16))
        bo_sb = stack.enter_context(nc.sbuf_tensor("bo_sb", [128, 16], F32))
        wst = [stack.enter_context(nc.sbuf_tensor(f"wst{i}", [128, NDC * 128], BF16))
               for i in range(2)]
        xold = stack.enter_context(nc.sbuf_tensor("xold", [128, 512], F32))
        yrow = stack.enter_context(nc.sbuf_tensor("yrow", [128, 512], F32))

        pbY = [cx.pb[0], cx.pb[1]]

        c_ev = None
        for c in range(NDC):
            c_ev = P.dma("sync", csb[:, c * T:(c + 1) * T], cfm[c * 128:(c + 1) * 128, :],
                         sem="ldo", tag=f"lo{c}")
        bo_ev = P.dma("sync", bo_sb[:, :], bo[:, :], sem="ldb", tag="bo")

        prev_w_cons = {}
        prev_read = {}
        prev_store = None
        prev_rev = None
        for fc in range(NDC):
            bsel = fc % 2
            wev = P.dma("gpsimd", wst[bsel].rearrange("p (c o) -> p c o", o=128),
                        Wo.rearrange("(c p) o -> p c o", p=128)[:, :, fc * 128:(fc + 1) * 128],
                        sem=f"lw{bsel}", waits=[prev_w_cons.get(bsel)], tag=f"w{fc}")
            for tb in range(4):
                bi = (fc * 4 + tb) % 2
                mev = None
                for c in range(NDC):
                    w = [wev, c_ev, prev_read.get(bi)] if c == 0 else []
                    mev = P.mm(pbY[bi], wst[bsel][:, c * 128:(c + 1) * 128],
                               csb[:, c * T + tb * 512: c * T + tb * 512 + 512],
                               start=(c == 0), stop=(c == NDC - 1), waits=w,
                               inc=P.sem("pm") if c == NDC - 1 else None, tag="pm")
                prev_w_cons[bsel] = mev
                lev = P.dma("sync", xold[:, :],
                            xres[fc * 128:(fc + 1) * 128, tb * 512:(tb + 1) * 512],
                            sem="ldr", waits=[prev_store], tag="lr")
                aev = P.add("scalar", lambda eng, bi=bi, fc=fc: eng.activation(
                    out=yrow[:, :], in_=pbY[bi][:, :], func=AF.Identity,
                    bias=bo_sb[:, fc:fc + 1]), waits=[mev, bo_ev, prev_rev], tag="bo")
                prev_read[bi] = aev
                rev = P.add("vector", lambda eng: eng.tensor_tensor(
                    out=xold[:, :], in0=xold[:, :], in1=yrow[:, :], op=OP.add),
                    waits=[lev, aev], tag="rs")
                prev_rev = rev
                prev_store = P.dma("sync",
                                   xout[fc * 128:(fc + 1) * 128, tb * 512:(tb + 1) * 512],
                                   xold[:, :], sem="stx", waits=[rev], tag="sx")
        P.add("scalar", lambda eng: eng.nop(), waits=[prev_store], tag="fin")
        P.emit()
    return nc


def run_att1(xshards, inp):
    Wqkv = np.ascontiguousarray(np.concatenate(
        [np.asarray(inp["L1_attn_Wq"]), np.asarray(inp["L1_attn_Wk"]),
         np.asarray(inp["L1_attn_Wv"])], axis=1))
    nc = build_ln_proj(6144, silu_hi=0)
    in_maps = [{"xres": xshards[r], "W": Wqkv} for r in range(NCORE)]
    res = run_bass_kernel_spmd(nc, in_maps, core_ids=list(range(NCORE)))
    qkv = [res.results[r]["yout"] for r in range(NCORE)]

    def gather(rows0, rows1, b):
        return np.concatenate([qkv[c][rows0:rows1, b * 1024:(b + 1) * 1024]
                               for c in range(NCORE)], axis=1)
    in_maps = []
    for r in range(NCORE):
        qh = np.empty((4, 128, 8192), dtype=ml_dtypes.bfloat16)
        kh = np.empty((4, 128, 8192), dtype=ml_dtypes.bfloat16)
        vth = np.empty((4, 8192, 128), dtype=ml_dtypes.bfloat16)
        for i, (b, h) in enumerate([(b, h) for b in (0, 1) for h in (2 * r, 2 * r + 1)]):
            qh[i] = gather(h * 128, (h + 1) * 128, b)
            kh[i] = gather(2048 + h * 128, 2048 + (h + 1) * 128, b)
            vth[i] = gather(4096 + h * 128, 4096 + (h + 1) * 128, b).T
        in_maps.append({"qh": qh, "kh": kh, "vth": np.ascontiguousarray(vth)})
    nc = build_att1_flash()
    res = run_bass_kernel_spmd(nc, in_maps, core_ids=list(range(NCORE)))
    ctxs = [res.results[r]["ctxh"] for r in range(NCORE)]

    c_full = np.empty((2048, 2, 8192), dtype=ml_dtypes.bfloat16)
    for r in range(NCORE):
        for i, (b, h) in enumerate([(b, h) for b in (0, 1) for h in (2 * r, 2 * r + 1)]):
            c_full[h * 128:(h + 1) * 128, b, :] = ctxs[r][i]
    bo_arr = np.ascontiguousarray(np.asarray(inp["L1_attn_bo"]).reshape(16, 128).T)
    Wo = np.ascontiguousarray(inp["L1_attn_Wo"])
    in_maps = []
    for r in range(NCORE):
        cfm = np.concatenate([c_full[:, 0, r * 1024:(r + 1) * 1024],
                              c_full[:, 1, r * 1024:(r + 1) * 1024]], axis=1)
        in_maps.append({"xres": xshards[r], "cfm": np.ascontiguousarray(cfm),
                        "Wo": Wo, "bo": bo_arr})
    nc = build_att1_post()
    res = run_bass_kernel_spmd(nc, in_maps, core_ids=list(range(NCORE)))
    return [res.results[r]["xout"] for r in range(NCORE)]


def kernel(**inputs):
    xs = _shard_x(np.asarray(inputs["L0_attn_x"], dtype=np.float32))
    xs = run_ffn0(xs, inputs)
    xs = run_att0(xs, inputs)
    xs = run_ffn0(xs, inputs)
    xs = run_ffn1(xs, inputs)
    xs = run_att1(xs, inputs)
    xs = run_ffn1(xs, inputs)
    return _unshard_x(xs)
